# version 06
# baseline (speedup 1.0000x reference)
import sys
sys.path.insert(0, '/opt/trn_rl_repo')
import contextlib
import numpy as np
import ml_dtypes
import concourse.bass as bass
import concourse.mybir as mybir
from concourse.bass_utils import run_bass_kernel_spmd

F32 = mybir.dt.float32
BF16 = mybir.dt.bfloat16
I32 = mybir.dt.int32
AL = mybir.AluOpType
AF = mybir.ActivationFunctionType
AX = mybir.AxisListType

T = 2048
D = 1024
NT = 16
ND = 8
EPS = 1e-5
MAGIC = float(np.float32(1.5 * 2 ** 23))
TWO_PI = 6.283185307179586
DEBUG = False
STAGES = 99


def brd(ap, dims, off_elems=0):
    return bass.AP(ap.tensor, ap.offset + off_elems, [list(d) for d in dims])


def pstride(ap):
    return list(ap.ap)[0][0]


def rep16(ap2d, nb):
    return bass.AP(ap2d.tensor, ap2d.offset, [list(ap2d.ap)[0], [1, nb], [0, 16]])


def build(stages=STAGES, debug=DEBUG):
    stages = float(stages)
    nc = bass.Bass("TRN2", target_bir_lowering=False, debug=False)
    sem = {}
    semh = {}
    for name in ("dsy", "dg", "dsc", "pe", "v", "sc"):
        semh[name] = nc.alloc_semaphore(name)
        sem[name] = 0
    dr = {}

    def din(name, shape, dt=F32):
        dr[name] = nc.dram_tensor(name, list(shape), dt, kind="ExternalInput").ap()

    din("x_in", [T, D])
    for nm, sh in [("w_kv", (128, 16384)), ("w_q", (128, 8192)), ("w_o0", (128, 8192)),
                   ("b_kv", (128, 16)), ("b_q", (128, 8)), ("b_o0", (128, 8)),
                   ("kvn_w", (128, 8)), ("kvn_b", (128, 8)), ("qn_w", (128, 8)),
                   ("qn_b", (128, 8)), ("ln_w", (1, 64)), ("ln_b", (1, 64)),
                   ("rh_w", (1, 1024)), ("rh_b", (1, 1)),
                   ("w_g0", (128, 22528)), ("w_u0", (128, 22528)), ("w_d0", (128, 22528)),
                   ("w_qkv1", (128, 24576)), ("b_qkv1", (128, 24)),
                   ("w_o1", (128, 8192)), ("b_o1", (128, 8)),
                   ("w_st", (128, 512)), ("lam", (128, 512)), ("gam", (128, 512))]:
        din(nm, sh)
    for nm in ("w_g1", "w_u1", "w_d1"):
        din(nm, (128, 24576), I32)
    x_out = nc.dram_tensor("x_out", [T, D], F32, kind="ExternalOutput").ap()
    spill = nc.dram_tensor("spill", [128, 22 * 2048], BF16, kind="Internal").ap()

    dbg_names = []

    def ddbg(name, shape, dt=BF16):
        if debug:
            dr[name] = nc.dram_tensor(name, list(shape), dt, kind="ExternalOutput").ap()
            dbg_names.append(name)

    ddbg("d_xln2", (128, 16384))
    ddbg("d_kfm", (128, 16384))
    ddbg("d_qfm", (128, 16384))
    ddbg("d_vfm", (128, 16384))
    ddbg("d_qn", (128, 16384))
    ddbg("d_kn", (128, 16384))
    ddbg("d_ofm", (128, 16384))
    ddbg("d_x1", (128, 16384), F32)
    ddbg("d_xq", (128, 16384))
    ddbg("d_x2", (128, 16384), F32)
    ddbg("d_qkv", (128, 49152))
    ddbg("d_y", (128, 16384))
    ddbg("d_x3", (128, 16384), F32)
    ddbg("d_rad", (128, 16), F32)
    ddbg("d_spill", (128, 45056))
    ddbg("d_wdt", (128, 22528))
    ddbg("d_stat", (128, 144), F32)

    mel_max = 2595.0 * np.log10(1.0 + 4000.0 / 200.0)
    fbv = np.power(np.float32(10.0), np.linspace(0.0, mel_max, 32, dtype=np.float32) / np.float32(2595.0)) - np.float32(1.0)
    fbv = (np.float32(200.0) * fbv / np.float32(1000.0)).astype(np.float32)
    c_fb = nc.inline_tensor(fbv.reshape(1, 32), name="c_fb").ap()
    c_iota = nc.inline_tensor(np.arange(128, dtype=np.float32).reshape(128, 1), name="c_iota").ap()
    c_id32 = nc.inline_tensor(np.eye(128, dtype=np.float32), name="c_id32").ap()
    c_id16 = nc.inline_tensor(np.eye(128, dtype=np.float32).astype(ml_dtypes.bfloat16), name="c_id16").ap()
    hmv = np.zeros((128, 2), np.float32)
    hmv[0:64, 0] = 1.0
    hmv[64:128, 1] = 1.0
    c_hmask = nc.inline_tensor(hmv.astype(ml_dtypes.bfloat16), name="c_hmask").ap()

    X = nc.alloc_sbuf_tensor("X", [128, NT * D], F32)
    BIG = nc.alloc_sbuf_tensor("BIG", [128, 49152], BF16)
    ID32 = nc.alloc_sbuf_tensor("ID32", [128, 128], F32)
    ID16 = nc.alloc_sbuf_tensor("ID16", [128, 128], BF16)
    STAT = nc.alloc_sbuf_tensor("STAT", [128, 144], F32)
    CV = nc.alloc_sbuf_tensor("CV", [128, 160], F32)
    CVB = nc.alloc_sbuf_tensor("CVB", [128, 16], BF16)
    HMASK = nc.alloc_sbuf_tensor("HMASK", [128, 2], BF16)
    SCC = nc.alloc_sbuf_tensor("SCC", [128, 400], F32)
    JUNK = nc.alloc_sbuf_tensor("JUNK", [128, 1024], F32)
    RHW = nc.alloc_sbuf_tensor("RHW", [128, 1024], F32)
    SEV = nc.alloc_sbuf_tensor("SEV", [128, 1024], BF16)
    PM = [nc.alloc_psum_tensor("PM%d" % i, [128, 512], F32) for i in range(6)]
    PT = [nc.alloc_psum_tensor("PT%d" % i, [128, 1024], BF16) for i in range(2)]


    with nc.Block() as block:
        SY, GP, SC_, VE, TE = block.sync, block.gpsimd, block.scalar, block.vector, block.tensor

        def inc(instr, name, n=1):
            instr.then_inc(semh[name], n)
            sem[name] += n
            return sem[name]

        def barrier():
            snap = {k: vv for k, vv in sem.items() if vv > 0}
            marks = {}

            def mk(e, own, sig):
                for nm2, val in snap.items():
                    e.wait_ge(semh[nm2], val)
                if sig == "drain":
                    inc(e.drain(), own)
                elif sig == "nop":
                    inc(e.nop(), own)
            TE(lambda e: mk(e, "pe", "nop"))
            marks["pe"] = sem["pe"]
            VE(lambda e: mk(e, "v", "drain"))
            marks["v"] = sem["v"]
            SC_(lambda e: mk(e, "sc", "drain"))
            marks["sc"] = sem["sc"]
            for emit in (SY, GP):
                def _(e, snap=snap, m=dict(marks)):
                    for nm2, val in snap.items():
                        e.wait_ge(semh[nm2], val)
                    for nm2, val in m.items():
                        e.wait_ge(semh[nm2], val)
                emit(_)

        def dump(name, src_ap):
            if debug:
                barrier()

                def _(s, name=name, src_ap=src_ap):
                    inc(s.dma_start(out=dr[name], in_=src_ap), "dsy", 16)
                SY(_)

        def finish(stage_ok):
            if stage_ok:
                return False
            barrier()

            def _(s):
                inc(s.dma_start(out=x_out.rearrange("(a p) b -> p a b", p=128),
                                in_=X[:].rearrange("p (a b) -> p a b", a=NT)), "dsy", 16)
                s.wait_ge(semh["dsy"], sem["dsy"])
            SY(_)
            return True

        ts_free = {"pt": [("v", 0), ("v", 0)], "pm": [("v", 0), ("v", 0)]}
        ts_vp = [0]

        def tstream(jobs, f32=False, pre=None):
            banks = PM[4:6] if f32 else PT
            per = 4 if f32 else 8
            STRIPB = JUNK if f32 else SEV
            ident = ID32 if f32 else ID16
            freemark = ts_free["pm" if f32 else "pt"]
            groups = [jobs[i:i + per] for i in range(0, len(jobs), per)]
            for gi, grp in enumerate(groups):
                b = gi % 2

                def _(t, grp=grp, b=b, fm=freemark[b], pre=(pre if gi == 0 else None)):
                    if pre is not None:
                        t.wait_ge(semh[pre[0]], pre[1])
                    if fm[1] > 0:
                        t.wait_ge(semh[fm[0]], fm[1])
                    last = None
                    for ci, (src, _e) in enumerate(grp):
                        last = t.transpose(banks[b][0:128, ci * 128:(ci + 1) * 128],
                                           in_=src, identity=ident[:])
                    inc(last, "pe")
                TE(_)
                pmk = sem["pe"]
                needs_v = any(e[0] == "vp" for (_s, e) in grp)

                def _(s, grp=grp, b=b, pmk=pmk, pvv=(ts_vp[0] if needs_v else 0)):
                    s.wait_ge(semh["pe"], pmk)
                    if pvv > 0:
                        s.wait_ge(semh["v"], pvv)
                    for ci, (_s2, e) in enumerate(grp):
                        ch = banks[b][0:128, ci * 128:(ci + 1) * 128]
                        if e[0] == "sc":
                            e[1](s, ch)
                        else:
                            e[1](s, ch, STRIPB[:, ci * 128:(ci + 1) * 128])
                    inc(s.drain(), "sc")
                SC_(_)
                smk = sem["sc"]
                if needs_v:
                    def _(v, grp=grp, smk=smk):
                        v.wait_ge(semh["sc"], smk)
                        for ci, (_s2, e) in enumerate(grp):
                            if e[0] == "vp":
                                e[2](v, STRIPB[:, ci * 128:(ci + 1) * 128])
                        inc(v.drain(), "v")
                    VE(_)
                    freemark[b] = ("v", sem["v"])
                    ts_vp[0] = sem["v"]
                else:
                    freemark[b] = ("sc", smk)

        def emit_sq_stats(col_base):
            for it in range(NT):
                def _(s, it=it, vg=sem["v"]):
                    if vg > 0:
                        s.wait_ge(semh["v"], vg)
                    s.activation(out=JUNK[:], in_=X[:, it * D:(it + 1) * D], func=AF.Square)
                    inc(s.drain(), "sc")
                SC_(_)
                smk = sem["sc"]

                def _(v, it=it, smk=smk, cb=col_base):
                    v.wait_ge(semh["sc"], smk)
                    v.reduce_sum(out=STAT[:, cb + it:cb + it + 1], in_=JUNK[:], axis=AX.X)
                    inc(v.drain(), "v")
                VE(_)

        def _(s):
            inc(s.dma_start(out=X[:].rearrange("p (a b) -> p a b", a=NT),
                            in_=dr["x_in"].rearrange("(a p) b -> p a b", p=128)), "dsy", 16)
            inc(s.dma_start(out=ID32[:], in_=c_id32), "dsy", 16)
            inc(s.dma_start(out=ID16[:], in_=c_id16), "dsy", 16)
            inc(s.dma_start(out=HMASK[:], in_=c_hmask), "dsy", 16)
            for nm, c0, n in (("b_kv", 0, 16), ("b_q", 16, 8), ("b_o0", 24, 8),
                              ("kvn_w", 32, 8), ("kvn_b", 40, 8), ("qn_w", 48, 8),
                              ("qn_b", 56, 8), ("b_qkv1", 64, 24), ("b_o1", 88, 8)):
                inc(s.dma_start(out=CV[:, c0:c0 + n], in_=dr[nm]), "dsy", 16)
            for nm, c0 in (("ln_w", 96), ("ln_b", 97)):
                inc(s.dma_start(out=CV[:, c0:c0 + 1],
                                in_=brd(dr[nm], [[0, 2], [1, 64]])), "dsy", 16)
            inc(s.dma_start(out=CV[:, 100:101],
                            in_=brd(dr["rh_b"], [[0, 128], [1, 1]])), "dsy", 16)
            inc(s.dma_start(out=SCC[:, 0:32], in_=brd(c_fb, [[0, 128], [1, 32]])), "dsy", 16)
            inc(s.dma_start(out=SCC[:, 32:33], in_=c_iota), "dsy", 16)
            inc(s.dma_start(out=RHW[:], in_=brd(dr["rh_w"], [[0, 128], [1, 1024]])), "dsy", 16)
        SY(_)
        ld0 = sem["dsy"]
        FBR = SCC[:, 0:32]
        IOTA = SCC[:, 32:33]

        def _(v):
            v.wait_ge(semh["dsy"], ld0)
            v.tensor_scalar(out=CV[:, 98:99], in0=CV[:, 96:97], scalar1=0.125, scalar2=None, op0=AL.mult)
            v.tensor_scalar(out=CV[:, 99:100], in0=CV[:, 97:98], scalar1=0.125, scalar2=None, op0=AL.mult)
            v.tensor_scalar(out=CV[:, 100:101], in0=CV[:, 100:101], scalar1=0.01, scalar2=None, op0=AL.mult)
            v.tensor_copy(out=CVB[:, 0:8], in_=CV[:, 40:48])
            v.tensor_copy(out=CVB[:, 8:16], in_=CV[:, 56:64])
            v.reduce_sum(out=SCC[0:1, 33:34], in_=RHW[0:1, :], axis=AX.X)
            inc(v.drain(), "v")
        VE(_)
        vpre = sem["v"]

        def _(s):
            s.wait_ge(semh["v"], vpre)
            inc(s.dma_start(out=SCC[:, 34:35],
                            in_=brd(SCC[0:1, 33:34], [[400, 1], [0, 128], [1, 1]])), "dsy", 16)
        SY(_)
        barrier()
        if finish(stages > 0.2):
            return nc, dbg_names

        o_xq, o_k, o_q = 0, 16384, 32768

        def _(v):
            for it in range(NT):
                xt = X[:, it * D:(it + 1) * D]
                v.reduce_sum(out=STAT[:, it:it + 1], in_=xt, axis=AX.X)
                v.scalar_tensor_tensor(out=JUNK[:], in0=xt, scalar=1.0, in1=RHW[:],
                                       op0=AL.mult, op1=AL.mult)
                v.drain()
                v.reduce_sum(out=STAT[:, 80 + it:81 + it], in_=JUNK[:], axis=AX.X)
            inc(v.drain(), "v")
        VE(_)
        emit_sq_stats(96)
        if stages <= 0.35:
            dump("d_stat", STAT[:])
        if finish(stages > 0.35):
            return nc, dbg_names

        def _(v):
            v.tensor_scalar(out=STAT[:, 0:16], in0=STAT[:, 0:16], scalar1=1.0 / 1024.0,
                            scalar2=None, op0=AL.mult)
            v.tensor_scalar(out=STAT[:, 96:112], in0=STAT[:, 96:112], scalar1=1.0 / 1024.0,
                            scalar2=None, op0=AL.mult)
            v.drain()
            v.tensor_tensor(out=STAT[:, 112:128], in0=STAT[:, 0:16], in1=STAT[:, 0:16], op=AL.mult)
            v.drain()
            v.tensor_tensor(out=STAT[:, 96:112], in0=STAT[:, 96:112], in1=STAT[:, 112:128], op=AL.subtract)
            v.drain()
            M1, V1 = STAT[:, 0:16], STAT[:, 96:112]
            T1, T2 = STAT[:, 112:128], STAT[:, 128:144]
            v.tensor_scalar(out=T1, in0=V1, scalar1=EPS, scalar2=None, op0=AL.add)
            v.drain()
            v.reciprocal(out=T1, in_=T1)
            v.drain()
            v.tensor_tensor(out=T2, in0=V1, in1=T1, op=AL.mult)
            v.drain()
            v.tensor_scalar(out=T2, in0=T2, scalar1=EPS, scalar2=None, op0=AL.add)
            v.drain()
            v.reciprocal(out=T2, in_=T2)
            v.drain()
            v.tensor_tensor(out=STAT[:, 16:32], in0=T1, in1=T2, op=AL.mult)
            inc(v.drain(), "v")
        VE(_)
        vA = sem["v"]

        def _(s):
            s.wait_ge(semh["v"], vA)
            s.sqrt(out=STAT[:, 112:128], in_=STAT[:, 112:128])
            s.sqrt(out=STAT[:, 16:32], in_=STAT[:, 16:32])
            inc(s.drain(), "sc")
        SC_(_)
        scA = sem["sc"]

        def _(v):
            v.wait_ge(semh["sc"], scA)
            TMP = STAT[:, 128:144]
            v.tensor_scalar(out=TMP, in0=STAT[:, 0:16], scalar1=SCC[:, 34:35], scalar2=None, op0=AL.mult)
            v.drain()
            v.tensor_tensor(out=TMP, in0=STAT[:, 80:96], in1=TMP, op=AL.subtract)
            v.drain()
            v.tensor_tensor(out=TMP, in0=TMP, in1=STAT[:, 112:128], op=AL.mult)
            v.drain()
            v.tensor_scalar(out=TMP, in0=TMP, scalar1=0.01, scalar2=CV[:, 100:101], op0=AL.mult, op1=AL.add)
            v.drain()
            v.tensor_scalar(out=STAT[:, 32:48], in0=TMP, scalar1=0.5, scalar2=2.0,
                            op0=AL.max, op1=AL.min)
            inc(v.drain(), "v")
        VE(_)
        barrier()
        dump("d_rad", STAT[:, 32:48])
        dump("d_stat", STAT[:])
        if finish(stages > 0.5):
            return nc, dbg_names

        with contextlib.ExitStack() as stA:
            HTOK = stA.enter_context(nc.sbuf_tensor("HTOKA", [128, 1024], BF16))
            for it in range(NT):
                def _(v, it=it, pg=sem["pe"]):
                    if pg > 0:
                        v.wait_ge(semh["pe"], pg)
                    v.tensor_scalar(out=HTOK[:], in0=X[:, it * D:(it + 1) * D],
                                    scalar1=STAT[:, it:it + 1], scalar2=STAT[:, 16 + it:17 + it],
                                    op0=AL.subtract, op1=AL.mult)
                    inc(v.drain(), "v")
                VE(_)
                vmk = sem["v"]
                jobs = []
                for jf in range(8):
                    def ev(s, chunk, jf=jf, it=it):
                        s.copy(out=BIG[:, o_xq + jf * 2048 + it * 128:
                                       o_xq + jf * 2048 + it * 128 + 128], in_=chunk)
                    jobs.append((HTOK[:, jf * 128:(jf + 1) * 128], ("sc", ev)))
                tstream(jobs, pre=("v", vmk))
            barrier()
        dump("d_xln2", BIG[:, o_xq:o_xq + 16384])
        if finish(stages > 1):
            return nc, dbg_names

        def xln_rhs(kt, ncc):
            return BIG[:, o_xq + kt * 2048 + ncc * 512: o_xq + kt * 2048 + ncc * 512 + 512]

        def stream_proj(w_name, nf, ktn, rhs_fn, dst_fn, bias_c0=None,
                        fold_c0=None, mv=None):
            with contextlib.ExitStack() as stk:
                WSL = [stk.enter_context(nc.sbuf_tensor("W_%s%d" % (w_name, i), [128, ktn * 128], BF16))
                       for i in range(2)]
                busy = [None, None]
                ev_done = [0, 0]
                for ft in range(nf):
                    b = ft % 2

                    def _(g, ft=ft, b=b, bz=busy[b]):
                        if bz is not None:
                            g.wait_ge(semh["pe"], bz)
                        inc(g.dma_start(out=WSL[b][:],
                                        in_=dr[w_name][:, ft * ktn * 128:(ft + 1) * ktn * 128]), "dg", 16)
                    GP(_)
                    gate = ("dg", sem["dg"])
                    if mv is not None:
                        bc0, pmi = mv

                        def _(t, ft=ft, b=b, g0=gate, bc0=bc0, pmi=pmi):
                            t.wait_ge(semh[g0[0]], g0[1])
                            last = None
                            for kt in range(ktn):
                                last = t.matmul(PM[pmi][0:128, ft:ft + 1],
                                                lhsT=WSL[b][:, kt * 128:(kt + 1) * 128],
                                                rhs=CVB[:, bc0 + kt:bc0 + kt + 1],
                                                start=(kt == 0), stop=(kt == ktn - 1))
                            inc(last, "pe")
                        TE(_)
                        gate = ("pe", sem["pe"])
                    if fold_c0 is not None:
                        def _(v, b=b, g0=gate, fc=fold_c0, ktn=ktn):
                            v.wait_ge(semh[g0[0]], g0[1])
                            for kt in range(ktn):
                                v.tensor_scalar(out=WSL[b][:, kt * 128:(kt + 1) * 128],
                                                in0=WSL[b][:, kt * 128:(kt + 1) * 128],
                                                scalar1=CV[:, fc + kt:fc + kt + 1],
                                                scalar2=None, op0=AL.mult)
                            inc(v.drain(), "v")
                        VE(_)
                        gate = ("v", sem["v"])
                    for ncc in range(4):
                        pmb = ncc % 2

                        def _(t, b=b, ncc=ncc, pmb=pmb, g0=gate, evd=ev_done[pmb], ktn=ktn):
                            t.wait_ge(semh[g0[0]], g0[1])
                            if evd > 0:
                                t.wait_ge(semh["sc"], evd)
                            last = None
                            for kt in range(ktn):
                                last = t.matmul(PM[pmb][:],
                                                lhsT=WSL[b][:, kt * 128:(kt + 1) * 128],
                                                rhs=rhs_fn(kt, ncc),
                                                start=(kt == 0), stop=(kt == ktn - 1))
                            inc(last, "pe")
                        TE(_)
                        pmk = sem["pe"]

                        def _(s, ft=ft, ncc=ncc, pmb=pmb, pmk=pmk):
                            s.wait_ge(semh["pe"], pmk)
                            if bias_c0 is not None:
                                s.activation(out=dst_fn(ft, ncc), in_=PM[pmb][:],
                                             func=AF.Identity,
                                             bias=CV[:, bias_c0 + ft:bias_c0 + ft + 1], scale=1.0)
                            else:
                                s.copy(out=dst_fn(ft, ncc), in_=PM[pmb][:])
                            ev_done[pmb] = inc(s.drain(), "sc")
                        SC_(_)
                    busy[b] = sem["pe"]
                barrier()

        stream_proj("w_q", 8, 8, xln_rhs,
                    lambda ft, ncc: BIG[:, o_q + ft * 2048 + ncc * 512: o_q + ft * 2048 + ncc * 512 + 512],
                    fold_c0=48, mv=(8, 3))
        stream_proj("w_kv", 8, 8, xln_rhs,
                    lambda ft, ncc: BIG[:, o_k + ft * 2048 + ncc * 512: o_k + ft * 2048 + ncc * 512 + 512],
                    fold_c0=32, mv=(0, 2))
        with contextlib.ExitStack() as stV:
            VWS = stV.enter_context(nc.sbuf_tensor("VWS", [128, 8 * 1024], BF16))
            VH = stV.enter_context(nc.sbuf_tensor("VH", [128, 8 * 512], BF16))

            def _(g):
                inc(g.dma_start(out=VWS[:], in_=dr["w_kv"][:, 8 * 8 * 128:16 * 8 * 128]), "dg", 16)
            GP(_)
            dmk = sem["dg"]

            def _(t, dmk=dmk):
                t.wait_ge(semh["dg"], dmk)
                last = None
                for f8 in range(8):
                    for kt in range(8):
                        last = t.matmul(PM[2][0:128, 8 + f8:9 + f8],
                                        lhsT=VWS[:, f8 * 1024 + kt * 128: f8 * 1024 + (kt + 1) * 128],
                                        rhs=CVB[:, kt:kt + 1],
                                        start=(kt == 0), stop=(kt == 7))
                inc(last, "pe")
            TE(_)
            pmk = sem["pe"]

            def _(v, pmk=pmk):
                v.wait_ge(semh["pe"], pmk)
                for f8 in range(8):
                    for kt in range(8):
                        v.tensor_scalar(out=VWS[:, f8 * 1024 + kt * 128: f8 * 1024 + (kt + 1) * 128],
                                        in0=VWS[:, f8 * 1024 + kt * 128: f8 * 1024 + (kt + 1) * 128],
                                        scalar1=CV[:, 32 + kt:33 + kt], scalar2=None, op0=AL.mult)
                inc(v.drain(), "v")
            VE(_)
            vws_ready = sem["v"]
            def _(s, pmk=pmk):
                s.wait_ge(semh["pe"], pmk)
                s.copy(out=SCC[:, 50:66], in_=PM[2][0:128, 0:16])
                s.copy(out=SCC[:, 66:74], in_=PM[3][0:128, 0:8])
                inc(s.drain(), "sc")
            SC_(_)
            scb = sem["sc"]

            def _(v, scb=scb):
                v.wait_ge(semh["sc"], scb)
                v.tensor_tensor(out=CV[:, 101:117], in0=SCC[:, 50:66], in1=CV[:, 0:16], op=AL.add)
                v.tensor_tensor(out=CV[:, 117:125], in0=SCC[:, 66:74], in1=CV[:, 16:24], op=AL.add)
                inc(v.drain(), "v")
            VE(_)
            bias_ready = sem["v"]
            copy_done = 0
            for ncc in range(4):
                ev_marks = []
                for f8 in range(8):
                    pmb = f8 % 2

                    def _(t, f8=f8, ncc=ncc, pmb=pmb, vr=vws_ready, cd=copy_done,
                          prev=(ev_marks[-2] if len(ev_marks) >= 2 else 0)):
                        t.wait_ge(semh["v"], vr)
                        if cd > 0:
                            t.wait_ge(semh["v"], cd)
                        if prev > 0:
                            t.wait_ge(semh["sc"], prev)
                        last = None
                        for kt in range(8):
                            last = t.matmul(PM[pmb][:],
                                            lhsT=VWS[:, f8 * 1024 + kt * 128: f8 * 1024 + (kt + 1) * 128],
                                            rhs=xln_rhs(kt, ncc),
                                            start=(kt == 0), stop=(kt == 7))
                        inc(last, "pe")
                    TE(_)
                    pmk2 = sem["pe"]

                    def _(s, f8=f8, pmb=pmb, pmk2=pmk2):
                        s.wait_ge(semh["pe"], pmk2)
                        s.copy(out=VH[:, f8 * 512:(f8 + 1) * 512], in_=PM[pmb][:])
                        inc(s.drain(), "sc")
                    SC_(_)
                    ev_marks.append(sem["sc"])
                vall = sem["sc"]

                def _(v, ncc=ncc, vall=vall, br=bias_ready):
                    v.wait_ge(semh["sc"], vall)
                    v.wait_ge(semh["v"], br)
                    for f8 in range(8):
                        v.tensor_scalar(out=BIG[:, o_xq + f8 * 2048 + ncc * 512:
                                                o_xq + f8 * 2048 + ncc * 512 + 512],
                                        in0=VH[:, f8 * 512:(f8 + 1) * 512],
                                        scalar1=CV[:, 109 + f8:110 + f8],
                                        scalar2=None, op0=AL.add)
                    inc(v.drain(), "v")
                VE(_)
                copy_done = sem["v"]
            barrier()
        o_v = o_xq

        def _(v):
            for ft in range(8):
                v.tensor_scalar(out=BIG[:, o_q + ft * 2048:o_q + (ft + 1) * 2048],
                                in0=BIG[:, o_q + ft * 2048:o_q + (ft + 1) * 2048],
                                scalar1=CV[:, 117 + ft:118 + ft], scalar2=None, op0=AL.add)
                v.tensor_scalar(out=BIG[:, o_k + ft * 2048:o_k + (ft + 1) * 2048],
                                in0=BIG[:, o_k + ft * 2048:o_k + (ft + 1) * 2048],
                                scalar1=CV[:, 101 + ft:102 + ft], scalar2=None, op0=AL.add)
            inc(v.drain(), "v")
        VE(_)
        barrier()
        dump("d_qfm", BIG[:, o_q:o_q + 16384])
        dump("d_kfm", BIG[:, o_k:o_k + 16384])
        dump("d_vfm", BIG[:, o_v:o_v + 16384])
        if finish(stages > 2):
            return nc, dbg_names

        with contextlib.ExitStack() as stC:
            HT = stC.enter_context(nc.sbuf_tensor("HT", [128, 1024], BF16))
            HR = stC.enter_context(nc.sbuf_tensor("HR", [128, 1024], F32))
            RT = stC.enter_context(nc.sbuf_tensor("RT", [128, 128], F32))
            FRT = stC.enter_context(nc.sbuf_tensor("FRT", [128, 64], F32))
            E1 = stC.enter_context(nc.sbuf_tensor("E1", [128, 512], F32))
            E2 = stC.enter_context(nc.sbuf_tensor("E2", [128, 512], F32))
            MQ = stC.enter_context(nc.sbuf_tensor("MQ", [128, 64], F32))
            for src_o, sc_c, bc_c in ((o_q, 96, 97), (o_k, 98, 99)):
                for it in range(NT):
                    jobs = []
                    for jf in range(8):
                        def ev(s, chunk, jf=jf):
                            s.copy(out=HT[:, jf * 128:(jf + 1) * 128], in_=chunk)
                        jobs.append((BIG[:, src_o + jf * 2048 + it * 128:
                                         src_o + jf * 2048 + it * 128 + 128], ("sc", ev)))
                    tstream(jobs)
                    vtok = sem["sc"]

                    def _(v, it=it, vtok=vtok):
                        v.wait_ge(semh["sc"], vtok)
                        v.tensor_scalar(out=SCC[:, 74:75], in0=IOTA, scalar1=float(it * 128),
                                        scalar2=None, op0=AL.add)
                        v.drain()
                        v.tensor_scalar(out=RT[:, 0:32], in0=FBR, scalar1=SCC[:, 74:75],
                                        scalar2=None, op0=AL.mult)
                        v.drain()
                        v.tensor_scalar(out=RT[:, 32:64], in0=RT[:, 0:32], scalar1=1.0 / TWO_PI,
                                        scalar2=MAGIC, op0=AL.mult, op1=AL.add)
                        v.drain()
                        v.tensor_scalar(out=RT[:, 32:64], in0=RT[:, 32:64], scalar1=MAGIC,
                                        scalar2=None, op0=AL.subtract)
                        v.drain()
                        v.scalar_tensor_tensor(out=RT[:, 32:64], in0=RT[:, 32:64], scalar=-TWO_PI,
                                               in1=RT[:, 0:32], op0=AL.mult, op1=AL.add)
                        v.tensor_scalar(out=RT[:, 96:128], in0=RT[:, 0:32], scalar1=float(np.pi / 2),
                                        scalar2=None, op0=AL.add)
                        v.drain()
                        v.tensor_scalar(out=RT[:, 64:96], in0=RT[:, 96:128], scalar1=1.0 / TWO_PI,
                                        scalar2=MAGIC, op0=AL.mult, op1=AL.add)
                        v.drain()
                        v.tensor_scalar(out=RT[:, 64:96], in0=RT[:, 64:96], scalar1=MAGIC,
                                        scalar2=None, op0=AL.subtract)
                        v.drain()
                        v.scalar_tensor_tensor(out=RT[:, 64:96], in0=RT[:, 64:96], scalar=-TWO_PI,
                                               in1=RT[:, 96:128], op0=AL.mult, op1=AL.add)
                        inc(v.drain(), "v")
                    VE(_)
                    vth = sem["v"]

                    def _(s, vth=vth):
                        s.wait_ge(semh["v"], vth)
                        s.activation(out=RT[:, 32:64], in_=RT[:, 32:64], func=AF.Sin)
                        s.activation(out=RT[:, 64:96], in_=RT[:, 64:96], func=AF.Sin)
                        inc(s.drain(), "sc")
                    SC_(_)
                    scs = sem["sc"]

                    def _(v, it=it, scs=scs):
                        v.wait_ge(semh["sc"], scs)
                        v.tensor_scalar(out=FRT[:, 0:32], in0=RT[:, 64:96],
                                        scalar1=STAT[:, 32 + it:33 + it], scalar2=None, op0=AL.mult)
                        v.tensor_scalar(out=FRT[:, 32:64], in0=RT[:, 32:64],
                                        scalar1=STAT[:, 32 + it:33 + it], scalar2=None, op0=AL.mult)
                        v.drain()
                        ae = brd(HT[:], [[1024, 128], [64, 16], [2, 32]])
                        ao = brd(HT[:], [[1024, 128], [64, 16], [2, 32]], 1)
                        fr = brd(FRT[:, 0:32], [[64, 128], [0, 16], [1, 32]])
                        fi = brd(FRT[:, 32:64], [[64, 128], [0, 16], [1, 32]])
                        e1 = E1[:].rearrange("p (a b) -> p a b", a=16)
                        e2 = E2[:].rearrange("p (a b) -> p a b", a=16)
                        hre = brd(HR[:], [[1024, 128], [64, 16], [2, 32]])
                        hro = brd(HR[:], [[1024, 128], [64, 16], [2, 32]], 1)
                        v.tensor_tensor(out=e1, in0=ae, in1=fr, op=AL.mult)
                        v.tensor_tensor(out=e2, in0=ao, in1=fi, op=AL.mult)
                        v.drain()
                        v.tensor_tensor(out=hre, in0=e1, in1=e2, op=AL.subtract)
                        v.drain()
                        v.tensor_tensor(out=e1, in0=ae, in1=fi, op=AL.mult)
                        v.tensor_tensor(out=e2, in0=ao, in1=fr, op=AL.mult)
                        v.drain()
                        v.tensor_tensor(out=hro, in0=e1, in1=e2, op=AL.add)
                        v.drain()
                        v.tensor_reduce(out=MQ[:, 0:16], in_=HR[:].rearrange("p (a b) -> p a b", a=16),
                                        axis=AX.X, op=AL.add)
                        v.drain()
                        v.tensor_scalar(out=MQ[:, 0:16], in0=MQ[:, 0:16], scalar1=1.0 / 64,
                                        scalar2=None, op0=AL.mult)
                        inc(v.drain(), "v")
                    VE(_)
                    vrot = sem["v"]

                    def _(s, vrot=vrot):
                        s.wait_ge(semh["v"], vrot)
                        s.activation(out=E1[:], in_=HR[:, 0:512], func=AF.Square)
                        s.activation(out=E2[:], in_=HR[:, 512:1024], func=AF.Square)
                        inc(s.drain(), "sc")
                    SC_(_)
                    scsq = sem["sc"]

                    def _(v, scsq=scsq):
                        v.wait_ge(semh["sc"], scsq)
                        v.tensor_reduce(out=MQ[:, 16:24], in_=E1[:].rearrange("p (a b) -> p a b", a=8),
                                        axis=AX.X, op=AL.add)
                        v.tensor_reduce(out=MQ[:, 24:32], in_=E2[:].rearrange("p (a b) -> p a b", a=8),
                                        axis=AX.X, op=AL.add)
                        v.drain()
                        v.tensor_scalar(out=MQ[:, 32:48], in0=MQ[:, 16:32], scalar1=1.0 / 64,
                                        scalar2=None, op0=AL.mult)
                        v.tensor_tensor(out=MQ[:, 48:64], in0=MQ[:, 0:16], in1=MQ[:, 0:16], op=AL.mult)
                        v.drain()
                        v.tensor_tensor(out=MQ[:, 32:48], in0=MQ[:, 32:48], in1=MQ[:, 48:64], op=AL.subtract)
                        v.drain()
                        v.tensor_scalar(out=MQ[:, 32:48], in0=MQ[:, 32:48], scalar1=EPS, scalar2=None, op0=AL.add)
                        v.drain()
                        v.reciprocal(out=MQ[:, 32:48], in_=MQ[:, 32:48])
                        inc(v.drain(), "v")
                    VE(_)
                    vv2 = sem["v"]

                    def _(s, vv2=vv2):
                        s.wait_ge(semh["v"], vv2)
                        s.sqrt(out=MQ[:, 32:48], in_=MQ[:, 32:48])
                        inc(s.drain(), "sc")
                    SC_(_)
                    scrs = sem["sc"]

                    def _(v, scrs=scrs):
                        v.wait_ge(semh["sc"], scrs)
                        mrep = brd(MQ[:, 0:16], [[64, 128], [1, 16], [0, 64]])
                        rrep = brd(MQ[:, 32:48], [[64, 128], [1, 16], [0, 64]])
                        hr3 = HR[:].rearrange("p (a b) -> p a b", a=16)
                        v.tensor_tensor(out=hr3, in0=hr3, in1=mrep, op=AL.subtract)
                        v.drain()
                        v.tensor_tensor(out=HT[:].rearrange("p (a b) -> p a b", a=16),
                                        in0=hr3, in1=rrep, op=AL.mult)
                        inc(v.drain(), "v")
                    VE(_)
                    vcs = sem["v"]
                    jobs = []
                    for jf in range(8):
                        def ev(s, chunk, jf=jf, it=it, so=src_o, scc=sc_c, bcc=bc_c):
                            s.activation(out=BIG[:, so + jf * 2048 + it * 128:
                                                 so + jf * 2048 + it * 128 + 128],
                                         in_=chunk, func=AF.Identity,
                                         scale=CV[:, scc:scc + 1], bias=CV[:, bcc:bcc + 1])
                        jobs.append((HT[:, jf * 128:(jf + 1) * 128], ("sc", ev)))
                    tstream(jobs, pre=("v", vcs))
            barrier()
        dump("d_qn", BIG[:, o_q:o_q + 16384])
        dump("d_kn", BIG[:, o_k:o_k + 16384])
        if finish(stages > 3):
            return nc, dbg_names

        with contextlib.ExitStack() as stN:
            SQB = stN.enter_context(nc.sbuf_tensor("SQB", [128, 2048], BF16))
            nrm_vdone = 0
            for si, src_o2 in enumerate((o_q, o_k)):
                for hp in range(8):
                    def _(v, hp=hp, src_o2=src_o2, pg=sem["pe"]):
                        if pg > 0:
                            v.wait_ge(semh["pe"], pg)
                        sl = BIG[:, src_o2 + hp * 2048:src_o2 + (hp + 1) * 2048]
                        v.tensor_tensor(out=SQB[:], in0=sl, in1=sl, op=AL.mult)
                        inc(v.drain(), "v")
                    VE(_)
                    vsq = sem["v"]
                    ev_prev = 0
                    for c in range(4):
                        def _(t, c=c, vsq=vsq, evp=ev_prev):
                            t.wait_ge(semh["v"], vsq)
                            if evp > 0:
                                t.wait_ge(semh["sc"], evp)
                            inc(t.matmul(PM[4][0:2, :], lhsT=HMASK[:, 0:2],
                                         rhs=SQB[:, c * 512:(c + 1) * 512],
                                         start=True, stop=True), "pe")
                        TE(_)
                        pnk = sem["pe"]

                        def _(s, c=c, pnk=pnk, nvd=(nrm_vdone if c == 0 else 0)):
                            s.wait_ge(semh["pe"], pnk)
                            if nvd > 0:
                                s.wait_ge(semh["v"], nvd)
                            r0j = 0 if c < 2 else 32
                            s.copy(out=JUNK[r0j:r0j + 2, (c % 2) * 512:(c % 2) * 512 + 512],
                                   in_=PM[4][0:2, :])
                            inc(s.drain(), "sc")
                        SC_(_)
                        ev_prev = sem["sc"]
                    scn = sem["sc"]

                    def _(v, scn=scn):
                        v.wait_ge(semh["sc"], scn)
                        v.reduce_max(out=SCC[0:2, 380:381], in_=JUNK[0:2, 0:1024], axis=AX.X)
                        v.reduce_max(out=SCC[32:34, 381:382], in_=JUNK[32:34, 0:1024], axis=AX.X)
                        inc(v.drain(), "v")
                    VE(_)
                    vrm = sem["v"]

                    def _(s, vrm=vrm):
                        s.wait_ge(semh["v"], vrm)
                        s.copy(out=SCC[0:2, 381:382], in_=SCC[32:34, 381:382])
                        inc(s.drain(), "sc")
                    SC_(_)
                    scc2 = sem["sc"]

                    def _(v, hp=hp, si=si, scc2=scc2):
                        v.wait_ge(semh["sc"], scc2)
                        cb = 340 + si * 8
                        v.tensor_tensor(out=SCC[0:2, cb + hp:cb + hp + 1], in0=SCC[0:2, 380:381],
                                        in1=SCC[0:2, 381:382], op=AL.max)
                        inc(v.drain(), "v")
                    VE(_)
                    nrm_vdone = sem["v"]
        vnrm = sem["v"]

        def _(v, vnrm=vnrm):
            v.tensor_tensor(out=SCC[0:2, 356:364], in0=SCC[0:2, 340:348],
                            in1=SCC[0:2, 348:356], op=AL.mult)
            inc(v.drain(), "v")
        VE(_)
        vb2 = sem["v"]

        def _(s, vb2=vb2):
            s.wait_ge(semh["v"], vb2)
            s.sqrt(out=SCC[0:2, 356:364], in_=SCC[0:2, 356:364])
            inc(s.drain(), "sc")
        SC_(_)
        scb2 = sem["sc"]

        def _(v, scb2=scb2):
            v.wait_ge(semh["sc"], scb2)
            v.tensor_scalar(out=SCC[0:2, 364:372], in0=SCC[0:2, 356:364], scalar1=-1.0,
                            scalar2=60.0, op0=AL.mult, op1=AL.add)
            inc(v.drain(), "v")
        VE(_)
        vbias = sem["v"]

        def _(s, vbias=vbias):
            s.wait_ge(semh["v"], vbias)
            for h in range(16):
                inc(s.dma_start(out=CV[:, 126 + h:127 + h],
                                in_=brd(SCC[h % 2:h % 2 + 1, 364 + h // 2:365 + h // 2],
                                        [[400, 1], [0, 128], [1, 1]])), "dsy", 16)
        SY(_)
        barrier()

        with contextlib.ExitStack() as stD:
            VAUG = stD.enter_context(nc.sbuf_tensor("VAUG", [128, 16 * 130], BF16))
            EXPS = [stD.enter_context(nc.sbuf_tensor("EXP%d" % i, [128, 512], BF16)) for i in range(2)]
            DN = stD.enter_context(nc.sbuf_tensor("DN", [128, 1024], F32))
            RD = stD.enter_context(nc.sbuf_tensor("RDE", [128, 1024], F32))
            OEV = stD.enter_context(nc.sbuf_tensor("OEV", [128, 1024], F32))
            for hp in range(8):
                def _(v, hp=hp, pg=sem["pe"]):
                    if pg > 0:
                        v.wait_ge(semh["pe"], pg)
                    v.memset(brd(VAUG[:], [[2080, 128], [65, 32], [1, 1]], 64), 1.0)
                    inc(v.drain(), "v")
                VE(_)
                vms = sem["v"]
                jobs = []
                for it in range(NT):
                    def ev(s, chunk, it=it):
                        s.copy(out=VAUG[:, it * 130 + 0:it * 130 + 64], in_=chunk[0:128, 0:64])
                        s.copy(out=VAUG[:, it * 130 + 65:it * 130 + 129], in_=chunk[0:128, 64:128])
                    jobs.append((BIG[:, o_v + hp * 2048 + it * 128: o_v + hp * 2048 + it * 128 + 128],
                                 ("sc", ev)))
                tstream(jobs, pre=("v", vms))
                barrier()
                oev_mark = 0
                for ncc in range(4):
                    sc_done = [0, 0]
                    av_prev = 0
                    for kt in range(NT):
                        sb = kt % 2

                        def _(t, hp=hp, ncc=ncc, kt=kt, sb=sb, scd=sc_done[sb], om=oev_mark):
                            if scd > 0:
                                t.wait_ge(semh["sc"], scd)
                            if kt == 0 and om > 0:
                                t.wait_ge(semh["sc"], om)
                            t.matmul(PM[0 + sb][:],
                                     lhsT=BIG[0:64, o_k + hp * 2048 + kt * 128: o_k + hp * 2048 + kt * 128 + 128],
                                     rhs=BIG[0:64, o_q + hp * 2048 + ncc * 512: o_q + hp * 2048 + ncc * 512 + 512],
                                     start=True, stop=True).then_inc(semh["pe"], 1)
                            sem["pe"] += 1
                            t.matmul(PM[2 + sb][:],
                                     lhsT=BIG[64:128, o_k + hp * 2048 + kt * 128: o_k + hp * 2048 + kt * 128 + 128],
                                     rhs=BIG[64:128, o_q + hp * 2048 + ncc * 512: o_q + hp * 2048 + ncc * 512 + 512],
                                     start=True, stop=True, tile_position=(64, 0)).then_inc(semh["pe"], 1)
                            sem["pe"] += 1
                        TE(_)
                        pmk = sem["pe"]

                        def _(s, sb=sb, pmk=pmk, avp=av_prev, hp=hp):
                            s.wait_ge(semh["pe"], pmk)
                            if avp > 0:
                                s.wait_ge(semh["pe"], avp)
                            s.activation(out=EXPS[0][:], in_=PM[0 + sb][:], func=AF.Exp,
                                         bias=CV[:, 126 + 2 * hp:127 + 2 * hp], scale=1.0)
                            s.activation(out=EXPS[1][:], in_=PM[2 + sb][:], func=AF.Exp,
                                         bias=CV[:, 127 + 2 * hp:128 + 2 * hp], scale=1.0)
                            inc(s.drain(), "sc")
                        SC_(_)
                        scmk = sem["sc"]
                        sc_done[sb] = scmk

                        def _(t, kt=kt, scmk=scmk):
                            t.wait_ge(semh["sc"], scmk)
                            t.matmul(PM[4][0:65, :], lhsT=VAUG[:, kt * 130:kt * 130 + 65],
                                     rhs=EXPS[0][:], start=(kt == 0), stop=(kt == 15)).then_inc(semh["pe"], 1)
                            sem["pe"] += 1
                            t.matmul(PM[5][0:65, :], lhsT=VAUG[:, kt * 130 + 65:kt * 130 + 130],
                                     rhs=EXPS[1][:], start=(kt == 0), stop=(kt == 15)).then_inc(semh["pe"], 1)
                            sem["pe"] += 1
                        TE(_)
                        av_prev = sem["pe"]
                    av_mark = sem["pe"]

                    def _(s, av_mark=av_mark):
                        s.wait_ge(semh["pe"], av_mark)
                        s.copy(out=DN[64:65, 0:512], in_=PM[4][64:65, :])
                        s.copy(out=DN[64:65, 512:1024], in_=PM[5][64:65, :])
                        inc(s.drain(), "sc")
                    SC_(_)
                    scdn = sem["sc"]

                    def _(v, scdn=scdn):
                        v.wait_ge(semh["sc"], scdn)
                        v.reciprocal(out=DN[64:65, 0:1024], in_=DN[64:65, 0:1024])
                        inc(v.drain(), "v")
                    VE(_)
                    vdn = sem["v"]

                    def _(s, vdn=vdn):
                        s.wait_ge(semh["v"], vdn)
                        inc(s.dma_start(out=RD[0:64, 0:1024],
                                        in_=brd(DN[64:65, 0:1024], [[1024, 1], [0, 64], [1, 1024]])), "dsc", 16)
                    SC_(_)
                    ddn = sem["dsc"]

                    def _(s, av_mark=av_mark):
                        s.copy(out=OEV[0:64, 0:512], in_=PM[4][0:64, :])
                        s.copy(out=OEV[0:64, 512:1024], in_=PM[5][0:64, :])
                        inc(s.drain(), "sc")
                    SC_(_)
                    scoe = sem["sc"]

                    def _(v, hp=hp, ncc=ncc, ddn=ddn, scoe=scoe):
                        v.wait_ge(semh["dsc"], ddn)
                        v.wait_ge(semh["sc"], scoe)
                        base = o_v + hp * 2048 + ncc * 512
                        v.tensor_tensor(out=BIG[0:64, base:base + 512], in0=OEV[0:64, 0:512],
                                        in1=RD[0:64, 0:512], op=AL.mult)
                        v.tensor_tensor(out=OEV[0:64, 512:1024], in0=OEV[0:64, 512:1024],
                                        in1=RD[0:64, 512:1024], op=AL.mult)
                        v.drain()
                        v.tensor_scalar(out=BIG[64:128, base:base + 512], in0=OEV[0:64, 512:1024],
                                        scalar1=1.0, scalar2=None, op0=AL.mult)
                        inc(v.drain(), "v")
                    VE(_)
                    oev_mark = scoe
            barrier()
        dump("d_ofm", BIG[:, o_v:o_v + 16384])

        def o_rhs(kt, ncc):
            return BIG[:, o_v + kt * 2048 + ncc * 512: o_v + kt * 2048 + ncc * 512 + 512]

        def proj_residual(w_name, bias_c0, rhs_fn, extra_scale=None):
            with contextlib.ExitStack() as stE:
                OFT = stE.enter_context(nc.sbuf_tensor("OFT_" + w_name, [128, 2048], F32))
                WSL = [stE.enter_context(nc.sbuf_tensor("WR%s%d" % (w_name, i), [128, 1024], BF16))
                       for i in range(2)]
                busy = [None, None]
                ev_done = [0, 0]
                for ft in range(8):
                    b = ft % 2

                    def _(g, ft=ft, b=b, bz=busy[b]):
                        if bz is not None:
                            g.wait_ge(semh["pe"], bz)
                        inc(g.dma_start(out=WSL[b][:], in_=dr[w_name][:, ft * 1024:(ft + 1) * 1024]), "dg", 16)
                    GP(_)
                    dmk = sem["dg"]
                    for ncc in range(4):
                        pmb = ncc % 2

                        def _(t, b=b, ncc=ncc, pmb=pmb, dmk=dmk, evd=ev_done[pmb]):
                            t.wait_ge(semh["dg"], dmk)
                            if evd > 0:
                                t.wait_ge(semh["sc"], evd)
                            last = None
                            for kt in range(8):
                                last = t.matmul(PM[pmb][:], lhsT=WSL[b][:, kt * 128:(kt + 1) * 128],
                                                rhs=rhs_fn(kt, ncc), start=(kt == 0), stop=(kt == 7))
                            inc(last, "pe")
                        TE(_)
                        pmk = sem["pe"]

                        def _(s, ft=ft, ncc=ncc, pmb=pmb, pmk=pmk):
                            s.wait_ge(semh["pe"], pmk)
                            if bias_c0 is not None:
                                s.activation(out=OFT[:, ncc * 512:(ncc + 1) * 512], in_=PM[pmb][:],
                                             func=AF.Identity,
                                             bias=CV[:, bias_c0 + ft:bias_c0 + ft + 1], scale=1.0)
                            else:
                                s.activation(out=OFT[:, ncc * 512:(ncc + 1) * 512], in_=PM[pmb][:],
                                             func=AF.Identity, bias=0.0, scale=extra_scale)
                            ev_done[pmb] = inc(s.drain(), "sc")
                        SC_(_)
                    voft = sem["sc"]
                    jobs = []
                    for it in range(NT):
                        def evs(s, chunk, strip):
                            s.copy(out=strip, in_=chunk)

                        def evv(v, strip, it=it, ft=ft):
                            v.tensor_tensor(out=X[:, it * 1024 + ft * 128: it * 1024 + ft * 128 + 128],
                                            in0=X[:, it * 1024 + ft * 128: it * 1024 + ft * 128 + 128],
                                            in1=strip, op=AL.add)
                        jobs.append((OFT[:, it * 128:(it + 1) * 128], ("vp", evs, evv)))
                    tstream(jobs, f32=True, pre=("sc", voft))
                    busy[b] = sem["pe"]
                barrier()

        proj_residual("w_o0", 24, o_rhs)
        dump("d_x1", X[:])
        if finish(stages > 4):
            return nc, dbg_names

        def emit_ln_stats():
            def _(v):
                for it in range(NT):
                    xt = X[:, it * D:(it + 1) * D]
                    v.reduce_sum(out=STAT[:, 48 + it:49 + it], in_=xt, axis=AX.X)
                inc(v.drain(), "v")
            VE(_)
            emit_sq_stats(64)

            def _(v):
                RS = STAT[:, 64:80]
                v.tensor_scalar(out=STAT[:, 48:64], in0=STAT[:, 48:64], scalar1=1.0 / 1024.0,
                                scalar2=None, op0=AL.mult)
                v.tensor_scalar(out=RS, in0=RS, scalar1=1.0 / 1024.0, scalar2=None, op0=AL.mult)
                v.drain()
                v.tensor_tensor(out=SCC[:, 36:52], in0=STAT[:, 48:64], in1=STAT[:, 48:64], op=AL.mult)
                v.drain()
                v.tensor_tensor(out=RS, in0=RS, in1=SCC[:, 36:52], op=AL.subtract)
                v.drain()
                v.tensor_scalar(out=RS, in0=RS, scalar1=EPS, scalar2=None, op0=AL.add)
                v.drain()
                v.reciprocal(out=RS, in_=RS)
                inc(v.drain(), "v")
            VE(_)
            vmk = sem["v"]

            def _(s, vmk=vmk):
                s.wait_ge(semh["v"], vmk)
                s.sqrt(out=STAT[:, 64:80], in_=STAT[:, 64:80])
                inc(s.drain(), "sc")
            SC_(_)

        emit_ln_stats()
        barrier()

        def emit_stair(v, AB, Tq, Tw, n):
            v.tensor_scalar(out=Tw, in0=AB, scalar1=2.0, scalar2=4.0, op0=AL.max, op1=AL.min)
            v.drain()
            v.tensor_scalar(out=Tw, in0=Tw, scalar1=MAGIC, scalar2=None, op0=AL.add)
            v.drain()
            v.tensor_scalar(out=Tw, in0=Tw, scalar1=MAGIC, scalar2=None, op0=AL.subtract)
            v.tensor_scalar(out=Tq, in0=AB, scalar1=2.0, scalar2=2.0, op0=AL.min, op1=AL.mult)
            v.drain()
            v.tensor_scalar(out=Tq, in0=Tq, scalar1=MAGIC, scalar2=None, op0=AL.add)
            v.drain()
            v.tensor_scalar(out=Tq, in0=Tq, scalar1=MAGIC, scalar2=None, op0=AL.subtract)
            v.drain()
            v.scalar_tensor_tensor(out=Tq, in0=Tq, scalar=0.5, in1=Tw, op0=AL.mult, op1=AL.add)
            v.tensor_scalar(out=Tw, in0=AB, scalar1=4.0, scalar2=6.0, op0=AL.max, op1=AL.min)
            v.drain()
            v.tensor_scalar(out=Tw, in0=Tw, scalar1=0.5, scalar2=MAGIC, op0=AL.mult, op1=AL.add)
            v.drain()
            v.tensor_scalar(out=Tw, in0=Tw, scalar1=MAGIC, scalar2=None, op0=AL.subtract)
            v.drain()
            v.scalar_tensor_tensor(out=Tq, in0=Tw, scalar=2.0, in1=Tq, op0=AL.mult, op1=AL.add)
            v.drain()
            v.tensor_scalar(out=Tq, in0=Tq, scalar1=6.0, scalar2=None, op0=AL.subtract)

        o_a, o_b = 0, 16384
        with contextlib.ExitStack() as stF:
            QS = stF.enter_context(nc.sbuf_tensor("QS", [128, 4 * 1024 + 128], F32))
            HT = stF.enter_context(nc.sbuf_tensor("HTF", [128, 1024], BF16))
            qst = 4 * 1024 + 128
            XN, SG, AB, TQ = QS[:, 0:1024], QS[:, 1024:2048], QS[:, 2048:3072], QS[:, 3072:4096]
            AMX, SCL = QS[:, 4096:4160], QS[:, 4160:4224]
            for it in range(NT):
                def _(v, it=it):
                    v.tensor_scalar(out=XN, in0=X[:, it * D:(it + 1) * D],
                                    scalar1=STAT[:, 48 + it:49 + it],
                                    scalar2=STAT[:, 64 + it:65 + it],
                                    op0=AL.subtract, op1=AL.mult)
                    v.drain()
                    v.tensor_reduce(out=AMX, in_=XN.rearrange("p (a b) -> p a b", b=16),
                                    axis=AX.X, op=AL.max, apply_absolute_value=True)
                    v.drain()
                    v.tensor_scalar(out=SCL, in0=AMX, scalar1=1.0 / 6.0, scalar2=1e-12,
                                    op0=AL.mult, op1=AL.max)
                    v.drain()
                    v.reciprocal(out=AMX, in_=SCL)
                    v.drain()
                    v.tensor_tensor(out=XN.rearrange("p (a b) -> p a b", b=16),
                                    in0=XN.rearrange("p (a b) -> p a b", b=16),
                                    in1=rep16(AMX, 64), op=AL.mult)
                    inc(v.drain(), "v")
                VE(_)
                vq1 = sem["v"]

                def _(s, vq1=vq1):
                    s.wait_ge(semh["v"], vq1)
                    s.activation(out=AB, in_=XN, func=AF.Abs)
                    s.sign(out=SG, in_=XN)
                    inc(s.drain(), "sc")
                SC_(_)
                scq = sem["sc"]

                def _(v, it=it, scq=scq, pg=sem["pe"]):
                    v.wait_ge(semh["sc"], scq)
                    if pg > 0:
                        v.wait_ge(semh["pe"], pg)
                    emit_stair(v, AB, TQ, XN, 1024)
                    v.tensor_tensor(out=SG.rearrange("p (a b) -> p a b", b=16),
                                    in0=SG.rearrange("p (a b) -> p a b", b=16),
                                    in1=rep16(SCL, 64), op=AL.mult)
                    v.drain()
                    v.tensor_tensor(out=HT[:], in0=TQ, in1=SG, op=AL.mult)
                    inc(v.drain(), "v")
                VE(_)
                vq2 = sem["v"]
                jobs = []
                for jf in range(8):
                    def ev(s, chunk, jf=jf, it=it):
                        s.copy(out=BIG[:, o_b + jf * 2048 + it * 128:
                                       o_b + jf * 2048 + it * 128 + 128], in_=chunk)
                    jobs.append((HT[:, jf * 128:(jf + 1) * 128], ("sc", ev)))
                tstream(jobs, pre=("v", vq2))
            barrier()
        dump("d_xq", BIG[:, o_b:o_b + 16384])
        if finish(stages > 5):
            return nc, dbg_names

        def xq_rhs(kt, ncc):
            return BIG[:, o_b + kt * 2048 + ncc * 512: o_b + kt * 2048 + ncc * 512 + 512]

        with contextlib.ExitStack() as stG:
            WG = stG.enter_context(nc.sbuf_tensor("WGF", [128, 1024], F32))
            QS = stG.enter_context(nc.sbuf_tensor("QSG", [128, 3 * 1024 + 256], F32))
            HN = stG.enter_context(nc.sbuf_tensor("HNG", [128, 512], F32))
            UPSF = stG.enter_context(nc.sbuf_tensor("UPSF", [128, 512], F32))
            HPF = stG.enter_context(nc.sbuf_tensor("HPF", [128, 2048], F32))
            HQB = BIG[:, 0:2048]
            WQT = [BIG[:, 2048:3072], BIG[:, 3072:4096]]
            WQN = BIG[:, 4096:5120]
            SG2, AB2, TQ2 = QS[:, 0:1024], QS[:, 1024:2048], QS[:, 2048:3072]
            AMX2, SCL2 = QS[:, 3072:3200], QS[:, 3200:3328]

            def wquant(src_name, ft, dst):
                wf = sem["v"]

                def _(g, src_name=src_name, ft=ft, wf=wf):
                    if wf > 0:
                        g.wait_ge(semh["v"], wf)
                    inc(g.dma_start(out=WG[:], in_=dr[src_name][:, ft * 1024:(ft + 1) * 1024]), "dg", 16)
                GP(_)
                dmk = sem["dg"]

                def _(v, dmk=dmk):
                    v.wait_ge(semh["dg"], dmk)
                    v.tensor_reduce(out=AMX2[:, 0:64], in_=WG[:].rearrange("p (a b) -> p a b", b=16),
                                    axis=AX.X, op=AL.max, apply_absolute_value=True)
                    v.drain()
                    v.tensor_scalar(out=SCL2[:, 0:64], in0=AMX2[:, 0:64], scalar1=1.0 / 6.0,
                                    scalar2=1e-12, op0=AL.mult, op1=AL.max)
                    v.drain()
                    v.reciprocal(out=AMX2[:, 0:64], in_=SCL2[:, 0:64])
                    v.drain()
                    v.tensor_tensor(out=WG[:].rearrange("p (a b) -> p a b", b=16),
                                    in0=WG[:].rearrange("p (a b) -> p a b", b=16),
                                    in1=rep16(AMX2[:, 0:64], 64), op=AL.mult)
                    inc(v.drain(), "v")
                VE(_)
                v1 = sem["v"]

                def _(s, v1=v1):
                    s.wait_ge(semh["v"], v1)
                    s.activation(out=AB2, in_=WG[:], func=AF.Abs)
                    s.sign(out=SG2, in_=WG[:])
                    inc(s.drain(), "sc")
                SC_(_)
                s1 = sem["sc"]

                def _(v, s1=s1, pg=sem["pe"]):
                    v.wait_ge(semh["sc"], s1)
                    if pg > 0:
                        v.wait_ge(semh["pe"], pg)
                    emit_stair(v, AB2, TQ2, WG[:], 1024)
                    v.tensor_tensor(out=SG2.rearrange("p (a b) -> p a b", b=16),
                                    in0=SG2.rearrange("p (a b) -> p a b", b=16),
                                    in1=rep16(SCL2[:, 0:64], 64), op=AL.mult)
                    v.drain()
                    v.tensor_tensor(out=WQN, in0=TQ2, in1=SG2, op=AL.mult)
                    inc(v.drain(), "v")
                VE(_)
                v2 = sem["v"]
                jobs = []
                for jd in range(8):
                    def ev(s, chunk, jd=jd, dst=dst):
                        s.copy(out=dst[:, jd * 128:(jd + 1) * 128], in_=chunk)
                    jobs.append((WQN[:, jd * 128:(jd + 1) * 128], ("sc", ev)))
                tstream(jobs, pre=("v", v2))
                return sem["sc"]

            for ft in range(22):
                wg_ready = wquant("w_g0", ft, WQT[0])
                wu_ready = wquant("w_u0", ft, WQT[1])
                h_marks = []
                for ncc in range(4):
                    def _(t, ncc=ncc, wg=wg_ready, wu=wu_ready, hm=(h_marks[-1] if h_marks else 0)):
                        t.wait_ge(semh["sc"], wg)
                        t.wait_ge(semh["sc"], wu)
                        if hm > 0:
                            t.wait_ge(semh["v"], hm)
                        last = None
                        for kt in range(8):
                            last = t.matmul(PM[0][:], lhsT=WQT[0][:, kt * 128:(kt + 1) * 128],
                                            rhs=xq_rhs(kt, ncc), start=(kt == 0), stop=(kt == 7))
                        inc(last, "pe")
                        last = None
                        for kt in range(8):
                            last = t.matmul(PM[1][:], lhsT=WQT[1][:, kt * 128:(kt + 1) * 128],
                                            rhs=xq_rhs(kt, ncc), start=(kt == 0), stop=(kt == 7))
                        inc(last, "pe")
                    TE(_)
                    pmk = sem["pe"]

                    def _(s, pmk=pmk):
                        s.wait_ge(semh["pe"], pmk)
                        s.activation(out=HN[:], in_=PM[0][:], func=AF.Silu)
                        s.copy(out=UPSF[:], in_=PM[1][:])
                        inc(s.drain(), "sc")
                    SC_(_)
                    smk = sem["sc"]

                    def _(v, ncc=ncc, smk=smk, pg=sem["pe"]):
                        v.wait_ge(semh["sc"], smk)
                        if pg > 0:
                            v.wait_ge(semh["pe"], pg)
                        v.tensor_tensor(out=HN[:], in0=HN[:], in1=UPSF[:], op=AL.mult)
                        inc(v.drain(), "v")
                    VE(_)
                    h_marks.append(sem["v"])
                    vhn = sem["v"]
                    jobs = []
                    for c in range(4):
                        itg = ncc * 4 + c

                        def ev(s, chunk, itg=itg):
                            s.copy(out=HPF[:, itg * 128:(itg + 1) * 128], in_=chunk)
                        jobs.append((HN[:, c * 128:(c + 1) * 128], ("sc", ev)))
                    tstream(jobs, f32=True, pre=("v", vhn))
                vhp = sem["sc"]
                for hh in range(2):
                    HPH = HPF[:, hh * 1024:(hh + 1) * 1024]

                    def _(v, vhp=vhp, HPH=HPH, hh=hh):
                        if hh == 0:
                            v.wait_ge(semh["sc"], vhp)
                        v.tensor_reduce(out=AMX2[:, 0:64], in_=HPH.rearrange("p (a b) -> p a b", b=16),
                                        axis=AX.X, op=AL.max, apply_absolute_value=True)
                        v.drain()
                        v.tensor_scalar(out=SCL2[:, 0:64], in0=AMX2[:, 0:64], scalar1=1.0 / 6.0,
                                        scalar2=1e-12, op0=AL.mult, op1=AL.max)
                        v.drain()
                        v.reciprocal(out=AMX2[:, 0:64], in_=SCL2[:, 0:64])
                        inc(v.drain(), "v")
                    VE(_)
                    v1 = sem["v"]

                    def _(s, v1=v1, HPH=HPH):
                        s.wait_ge(semh["v"], v1)
                        s.activation(out=AB2, in_=HPH, func=AF.Abs)
                        s.sign(out=SG2, in_=HPH)
                        inc(s.drain(), "sc")
                    SC_(_)
                    s1 = sem["sc"]

                    def _(v, s1=s1, hh=hh, pg=sem["pe"], dg_=sem["dsy"]):
                        v.wait_ge(semh["sc"], s1)
                        if pg > 0:
                            v.wait_ge(semh["pe"], pg)
                        if dg_ > 0:
                            v.wait_ge(semh["dsy"], dg_)
                        v.tensor_tensor(out=AB2.rearrange("p (a b) -> p a b", b=16),
                                        in0=AB2.rearrange("p (a b) -> p a b", b=16),
                                        in1=rep16(AMX2[:, 0:64], 64), op=AL.mult)
                        v.drain()
                        emit_stair(v, AB2, TQ2, WG[:], 1024)
                        v.tensor_tensor(out=SG2.rearrange("p (a b) -> p a b", b=16),
                                        in0=SG2.rearrange("p (a b) -> p a b", b=16),
                                        in1=rep16(SCL2[:, 0:64], 64), op=AL.mult)
                        v.drain()
                        v.tensor_tensor(out=HQB[:, hh * 1024:(hh + 1) * 1024], in0=TQ2, in1=SG2, op=AL.mult)
                        inc(v.drain(), "v")
                    VE(_)
                vhq = sem["v"]
                jobs = []
                for itg in range(16):
                    def ev(s, chunk, itg=itg):
                        s.copy(out=HQB[:, itg * 128:(itg + 1) * 128], in_=chunk)
                    jobs.append((HQB[:, itg * 128:(itg + 1) * 128], ("sc", ev)))
                tstream(jobs, pre=("v", vhq))
                vtb = sem["sc"]

                def _(s, ft=ft, vtb=vtb):
                    s.wait_ge(semh["sc"], vtb)
                    inc(s.dma_start(out=spill[:, ft * 2048:(ft + 1) * 2048], in_=HQB), "dsy", 16)
                SY(_)
            barrier()
        dump("d_spill", spill[:, 0:45056])

        with contextlib.ExitStack() as stH:
            WDT = BIG[:, 0:22 * 1024]
            WG = stH.enter_context(nc.sbuf_tensor("WGH", [128, 1024], F32))
            QS = stH.enter_context(nc.sbuf_tensor("QSH", [128, 4 * 1024 + 256], F32))
            WQN = stH.enter_context(nc.sbuf_tensor("WQNH", [128, 1024], BF16))
            HQS = [stH.enter_context(nc.sbuf_tensor("HQS%d" % i, [128, 512], BF16)) for i in range(2)]
            DOF = stH.enter_context(nc.sbuf_tensor("DOF", [128, 4 * 512], F32))
            XN3, SG3, AB3, TQ3 = QS[:, 0:1024], QS[:, 1024:2048], QS[:, 2048:3072], QS[:, 3072:4096]
            AMX3, SCL3 = QS[:, 4096:4160], QS[:, 4160:4224]
            for dt in range(8):
                for p0, pn in ((0, 1024), (1024, 1024), (2048, 768)):
                    nb = pn // 16
                    wf = sem["v"]

                    def _(g, dt=dt, p0=p0, pn=pn, wf=wf):
                        if wf > 0:
                            g.wait_ge(semh["v"], wf)
                        inc(g.dma_start(out=WG[:, 0:pn],
                                        in_=dr["w_d0"][:, dt * 2816 + p0: dt * 2816 + p0 + pn]), "dg", 16)
                    GP(_)
                    dmk = sem["dg"]

                    def _(v, dmk=dmk, pn=pn, nb=nb):
                        v.wait_ge(semh["dg"], dmk)
                        v.tensor_reduce(out=AMX3[:, 0:nb], in_=WG[:, 0:pn].rearrange("p (a b) -> p a b", b=16),
                                        axis=AX.X, op=AL.max, apply_absolute_value=True)
                        v.drain()
                        v.tensor_scalar(out=SCL3[:, 0:nb], in0=AMX3[:, 0:nb], scalar1=1.0 / 6.0,
                                        scalar2=1e-12, op0=AL.mult, op1=AL.max)
                        v.drain()
                        v.reciprocal(out=AMX3[:, 0:nb], in_=SCL3[:, 0:nb])
                        v.drain()
                        v.tensor_tensor(out=XN3[:, 0:pn].rearrange("p (a b) -> p a b", b=16),
                                        in0=WG[:, 0:pn].rearrange("p (a b) -> p a b", b=16),
                                        in1=rep16(AMX3[:, 0:nb], nb), op=AL.mult)
                        inc(v.drain(), "v")
                    VE(_)
                    v1 = sem["v"]

                    def _(s, v1=v1, pn=pn):
                        s.wait_ge(semh["v"], v1)
                        s.activation(out=AB3[:, 0:pn], in_=XN3[:, 0:pn], func=AF.Abs)
                        s.sign(out=SG3[:, 0:pn], in_=XN3[:, 0:pn])
                        inc(s.drain(), "sc")
                    SC_(_)
                    s1 = sem["sc"]

                    def _(v, s1=s1, pn=pn, nb=nb, pg=sem["pe"]):
                        v.wait_ge(semh["sc"], s1)
                        if pg > 0:
                            v.wait_ge(semh["pe"], pg)
                        emit_stair(v, AB3[:, 0:pn], TQ3[:, 0:pn], XN3[:, 0:pn], pn)
                        v.tensor_tensor(out=SG3[:, 0:pn].rearrange("p (a b) -> p a b", b=16),
                                        in0=SG3[:, 0:pn].rearrange("p (a b) -> p a b", b=16),
                                        in1=rep16(SCL3[:, 0:nb], nb), op=AL.mult)
                        v.drain()
                        v.tensor_tensor(out=WQN[:, 0:pn], in0=TQ3[:, 0:pn], in1=SG3[:, 0:pn], op=AL.mult)
                        inc(v.drain(), "v")
                    VE(_)
                    v2 = sem["v"]
                    jobs = []
                    for kk in range(pn // 128):
                        kt = p0 // 128 + kk

                        def ev(s, chunk, kt=kt, dt=dt):
                            s.copy(out=WDT[:, (kt * 8 + dt) * 128:(kt * 8 + dt) * 128 + 128], in_=chunk)
                        jobs.append((WQN[:, kk * 128:(kk + 1) * 128], ("sc", ev)))
                    tstream(jobs, pre=("v", v2))
            barrier()
            dump("d_wdt", WDT)
            res_done = 0
            hq_pe = [0, 0]
            for ncc in range(4):
                for ftg in range(2):
                    for kt in range(22):
                        hb = kt % 2

                        def _(s, kt=kt, ncc=ncc, hb=hb, prev=hq_pe[hb]):
                            if prev > 0:
                                s.wait_ge(semh["pe"], prev)
                            inc(s.dma_start(out=HQS[hb][:],
                                            in_=spill[:, kt * 2048 + ncc * 512: kt * 2048 + ncc * 512 + 512]),
                                "dsy", 16)
                        SY(_)
                        dmk = sem["dsy"]

                        def _(t, kt=kt, ftg=ftg, hb=hb, dmk=dmk, rd=res_done):
                            t.wait_ge(semh["dsy"], dmk)
                            if kt == 0 and rd > 0:
                                t.wait_ge(semh["sc"], rd)
                            last = None
                            for f4 in range(4):
                                ftd = ftg * 4 + f4
                                last = t.matmul(PM[f4][:],
                                                lhsT=WDT[:, (kt * 8 + ftd) * 128:(kt * 8 + ftd) * 128 + 128],
                                                rhs=HQS[hb][:], start=(kt == 0), stop=(kt == 21))
                            inc(last, "pe")
                        TE(_)
                        hq_pe[hb] = sem["pe"]
                    pmk = sem["pe"]

                    def _(s, pmk=pmk):
                        s.wait_ge(semh["pe"], pmk)
                        for f4 in range(4):
                            s.copy(out=DOF[:, f4 * 512:(f4 + 1) * 512], in_=PM[f4][:])
                        inc(s.drain(), "sc")
                    SC_(_)
                    vdo = sem["sc"]
                    res_done = vdo
                    jobs = []
                    for f4 in range(4):
                        for c in range(4):
                            it = ncc * 4 + c
                            ftd = ftg * 4 + f4

                            def evs(s, chunk, strip):
                                s.copy(out=strip, in_=chunk)

                            def evv(v, strip, it=it, ftd=ftd):
                                v.tensor_tensor(out=X[:, it * 1024 + ftd * 128: it * 1024 + ftd * 128 + 128],
                                                in0=X[:, it * 1024 + ftd * 128: it * 1024 + ftd * 128 + 128],
                                                in1=strip, op=AL.add)
                            jobs.append((DOF[:, f4 * 512 + c * 128: f4 * 512 + c * 128 + 128],
                                         ("vp", evs, evv)))
                    tstream(jobs, f32=True, pre=("sc", vdo))
            barrier()
        dump("d_x2", X[:])
        if finish(stages > 6):
            return nc, dbg_names

        emit_ln_stats()
        barrier()
        with contextlib.ExitStack() as stI:
            XH = stI.enter_context(nc.sbuf_tensor("XH", [128, 8 * 1024], BF16))
            HT = stI.enter_context(nc.sbuf_tensor("HTI", [128, 1024], BF16))
            WSL = [stI.enter_context(nc.sbuf_tensor("WQ1_%d" % i, [128, 1024], BF16)) for i in range(2)]
            for half in range(2):
                for itl in range(8):
                    it = half * 8 + itl

                    def _(v, it=it, pg=sem["pe"]):
                        if pg > 0:
                            v.wait_ge(semh["pe"], pg)
                        v.tensor_scalar(out=HT[:], in0=X[:, it * D:(it + 1) * D],
                                        scalar1=STAT[:, 48 + it:49 + it],
                                        scalar2=STAT[:, 64 + it:65 + it],
                                        op0=AL.subtract, op1=AL.mult)
                        inc(v.drain(), "v")
                    VE(_)
                    vmk = sem["v"]
                    jobs = []
                    for jf in range(8):
                        def ev(s, chunk, jf=jf, itl=itl):
                            s.copy(out=XH[:, jf * 1024 + itl * 128: jf * 1024 + itl * 128 + 128],
                                   in_=chunk)
                        jobs.append((HT[:, jf * 128:(jf + 1) * 128], ("sc", ev)))
                    tstream(jobs, pre=("v", vmk))
                barrier()
                busy = [None, None]
                ev_done = [0, 0]
                for ft in range(24):
                    b = ft % 2

                    def _(g, ft=ft, b=b, bz=busy[b]):
                        if bz is not None:
                            g.wait_ge(semh["pe"], bz)
                        inc(g.dma_start(out=WSL[b][:],
                                        in_=dr["w_qkv1"][:, ft * 1024:(ft + 1) * 1024]), "dg", 16)
                    GP(_)
                    dmk = sem["dg"]
                    for nh in range(2):
                        pmb = nh % 2

                        def _(t, b=b, nh=nh, pmb=pmb, dmk=dmk, evd=ev_done[pmb]):
                            t.wait_ge(semh["dg"], dmk)
                            if evd > 0:
                                t.wait_ge(semh["sc"], evd)
                            last = None
                            for kt in range(8):
                                last = t.matmul(PM[pmb][:], lhsT=WSL[b][:, kt * 128:(kt + 1) * 128],
                                                rhs=XH[:, kt * 1024 + nh * 512: kt * 1024 + nh * 512 + 512],
                                                start=(kt == 0), stop=(kt == 7))
                            inc(last, "pe")
                        TE(_)
                        pmk = sem["pe"]

                        def _(s, ft=ft, nh=nh, half=half, pmb=pmb, pmk=pmk):
                            s.wait_ge(semh["pe"], pmk)
                            dst = BIG[:, ft * 2048 + half * 1024 + nh * 512:
                                      ft * 2048 + half * 1024 + nh * 512 + 512]
                            s.activation(out=dst, in_=PM[pmb][:], func=AF.Identity,
                                         bias=CV[:, 64 + ft:65 + ft], scale=1.0)
                            ev_done[pmb] = inc(s.drain(), "sc")
                        SC_(_)
                    busy[b] = sem["pe"]
                barrier()
        dump("d_qkv", BIG[:])
        if finish(stages > 7):
            return nc, dbg_names

        with contextlib.ExitStack() as stJ:
            GC = stJ.enter_context(nc.sbuf_tensor("GC", [128, 512], F32))
            WSTB = stJ.enter_context(nc.sbuf_tensor("WSTB", [128, 512], BF16))
            LAMS = stJ.enter_context(nc.sbuf_tensor("LAMS", [128, 512], F32))
            RC = LAMS
            GST = stJ.enter_context(nc.sbuf_tensor("GST", [128, 2048], BF16))
            KD = stJ.enter_context(nc.sbuf_tensor("KD", [128, 2048], BF16))
            QD = stJ.enter_context(nc.sbuf_tensor("QD", [128, 2048], BF16))
            VR = [stJ.enter_context(nc.sbuf_tensor("VR%d" % i, [128, 2048], BF16)) for i in range(2)]
            ZB = [stJ.enter_context(nc.sbuf_tensor("ZB%d" % i, [128, 2048], BF16)) for i in range(2)]

            def _(g):
                inc(g.dma_start(out=LAMS[:], in_=dr["lam"]), "dg", 16)
                inc(g.dma_start(out=GC[:], in_=dr["gam"]), "dg", 16)
                inc(g.dma_start(out=WSTB[:], in_=dr["w_st"]), "dg", 16)
            GP(_)
            dld = sem["dg"]

            def _(s, dld=dld):
                s.wait_ge(semh["dg"], dld)
                s.activation(out=LAMS[:], in_=LAMS[:], func=AF.Sigmoid)
                inc(s.drain(), "sc")
            SC_(_)
            ssg = sem["sc"]

            def _(v, ssg=ssg):
                v.wait_ge(semh["sc"], ssg)
                v.tensor_scalar(out=RC[:], in0=LAMS[:], scalar1=-1.0, scalar2=1.0, op0=AL.mult, op1=AL.add)
                v.tensor_scalar(out=GC[:], in0=GC[:], scalar1=0.125, scalar2=None, op0=AL.mult)
                inc(v.drain(), "v")
            VE(_)
            barrier()
            kq_free = 0
            zc_used = [0, 0]
            vr_vmark = [0, 0]
            for h in range(16):
                r0 = (h % 2) * 64
                kofs = (8 + h // 2) * 2048
                qofs = (h // 2) * 2048
                vofs = (16 + h // 2) * 2048
                barrier()

                def _(s, h=h, r0=r0, kofs=kofs, qofs=qofs, kqf=kq_free):
                    if kqf > 0:
                        s.wait_ge(semh["v"], kqf)
                    for dst, src in ((KD, kofs), (QD, qofs)):
                        for sdup in range(2):
                            inc(s.dma_start(out=dst[sdup * 64:(sdup + 1) * 64, :],
                                            in_=BIG[r0:r0 + 64, src:src + 2048]), "dsy", 16)
                SY(_)
                ddup = sem["dsy"]
                gst_guard = sem["pe"]

                def _(v, h=h, gg=gst_guard):
                    if gg > 0:
                        v.wait_ge(semh["pe"], gg)
                    v.memset(GST[:], 0.0)
                    v.drain()
                    v.tensor_copy(out=brd(GST[0:64, :], [[2048, 64], [66, 32]]),
                                  in_=GC[0:64, h * 32:h * 32 + 32])
                    v.tensor_copy(out=brd(GST[64:128, :], [[2048, 64], [66, 32]], 1),
                                  in_=GC[64:128, h * 32:h * 32 + 32])
                    inc(v.drain(), "v")
                VE(_)
                vst = sem["v"]
                y_evac_done = 0 if h == 0 else y_evac_done
                for m in range(32):
                    vb = m % 2

                    def _(s, h=h, m=m, vb=vb, r0=r0, vofs=vofs, vru=vr_vmark[vb]):
                        if vru > 0:
                            s.wait_ge(semh["v"], vru)
                        for sdup in range(2):
                            row = r0 + 2 * m + sdup
                            inc(s.dma_start(out=VR[vb][sdup * 64:(sdup + 1) * 64, :],
                                            in_=brd(BIG[row:row + 1, vofs:vofs + 2048],
                                                    [[49152, 1], [0, 64], [1, 2048]])), "dsc", 16)
                    SC_(_)
                    dvr = sem["dsc"]
                    zb = m % 2

                    def _(v, h=h, m=m, vb=vb, zb=zb, dvr=dvr, ddup=ddup, zcu=zc_used[zb]):
                        v.wait_ge(semh["dsc"], dvr)
                        if m < 2:
                            v.wait_ge(semh["dsy"], ddup)
                        if zcu > 0:
                            v.wait_ge(semh["pe"], zcu)
                        v.tensor_tensor(out=ZB[zb][:], in0=VR[vb][:], in1=KD[:], op=AL.mult)
                        v.drain()
                        v.tensor_tensor_scan(out=ZB[zb][:], data0=brd(RC[:, h * 32 + m:h * 32 + m + 1],
                                                                     [[512, 128], [0, 2048]]),
                                             data1=ZB[zb][:], initial=0.0, op0=AL.mult, op1=AL.add)
                        v.drain()
                        v.tensor_tensor(out=ZB[zb][:], in0=ZB[zb][:], in1=QD[:], op=AL.mult)
                        inc(v.drain(), "v")
                    VE(_)
                    vr_vmark[vb] = sem["v"]
                    vz = sem["v"]

                    def _(t, h=h, m=m, zb=zb, vz=vz, vst=vst, yev=y_evac_done):
                        t.wait_ge(semh["v"], vz)
                        if m == 0:
                            t.wait_ge(semh["v"], vst)
                            if yev > 0:
                                t.wait_ge(semh["sc"], yev)
                        last = None
                        for ncc in range(4):
                            last = t.matmul(PM[ncc][0:64, :], lhsT=GST[:, m * 64:(m + 1) * 64],
                                            rhs=ZB[zb][:, ncc * 512:(ncc + 1) * 512],
                                            start=(m == 0), stop=False)
                        inc(last, "pe")
                    TE(_)
                    zc_used[zb] = sem["pe"]

                def _(t, h=h, r0=r0, qofs=qofs):
                    last = None
                    for ncc in range(4):
                        last = t.matmul(PM[ncc][0:64, :],
                                        lhsT=WSTB[r0:r0 + 64, (h // 2) * 64:(h // 2) * 64 + 64],
                                        rhs=BIG[r0:r0 + 64, qofs + ncc * 512: qofs + ncc * 512 + 512],
                                        start=False, stop=True)
                    inc(last, "pe")
                TE(_)
                pmy = sem["pe"]

                def _(s, h=h, r0=r0, qofs=qofs, pmy=pmy):
                    s.wait_ge(semh["pe"], pmy)
                    for ncc in range(4):
                        s.copy(out=BIG[r0:r0 + 64, qofs + ncc * 512: qofs + ncc * 512 + 512],
                               in_=PM[ncc][0:64, :])
                    inc(s.drain(), "sc")
                SC_(_)
                y_evac_done = sem["sc"]
                kq_free = vr_vmark[(31) % 2]
                kq_free = sem["v"]
            barrier()
        dump("d_y", BIG[:, 0:16384])

        proj_residual("w_o1", 88, lambda kt, ncc: BIG[:, kt * 2048 + ncc * 512: kt * 2048 + ncc * 512 + 512])
        dump("d_x3", X[:])
        if finish(stages > 8):
            return nc, dbg_names

        emit_ln_stats()
        barrier()
        if finish(stages > 8.2):
            return nc, dbg_names
        o_h, o_wd = 0, 24576
        with contextlib.ExitStack() as stL:
            XH = stL.enter_context(nc.sbuf_tensor("XH1", [128, 8 * 1024], BF16))
            HT = stL.enter_context(nc.sbuf_tensor("HTL", [128, 1024], BF16))
            WSL = [stL.enter_context(nc.sbuf_tensor("WF1_%d" % i, [128, 1024], BF16)) for i in range(2)]
            WSU = [stL.enter_context(nc.sbuf_tensor("WF2_%d" % i, [128, 1024], BF16)) for i in range(2)]
            SIL = stL.enter_context(nc.sbuf_tensor("SIL1", [128, 512], BF16))
            UPS = stL.enter_context(nc.sbuf_tensor("UPS1", [128, 512], BF16))
            DOF = stL.enter_context(nc.sbuf_tensor("DOF1", [128, 2048], BF16))

            def _(g):
                for wc in range(24):
                    inc(g.dma_start(out=BIG[:, o_wd + wc * 1024:o_wd + (wc + 1) * 1024],
                                    in_=dr["w_d1"][:, wc * 1024:(wc + 1) * 1024]), "dg", 16)
            GP(_)
            wd_ready = sem["dg"]
            for half in range(2):
                for itl in range(8):
                    it = half * 8 + itl

                    def _(v, it=it, pg=sem["pe"]):
                        if pg > 0:
                            v.wait_ge(semh["pe"], pg)
                        v.tensor_scalar(out=HT[:], in0=X[:, it * D:(it + 1) * D],
                                        scalar1=STAT[:, 48 + it:49 + it],
                                        scalar2=STAT[:, 64 + it:65 + it],
                                        op0=AL.subtract, op1=AL.mult)
                        inc(v.drain(), "v")
                    VE(_)
                    vmk = sem["v"]
                    jobs = []
                    for jf in range(8):
                        def ev(s, chunk, jf=jf, itl=itl):
                            s.copy(out=XH[:, jf * 1024 + itl * 128: jf * 1024 + itl * 128 + 128],
                                   in_=chunk)
                        jobs.append((HT[:, jf * 128:(jf + 1) * 128], ("sc", ev)))
                    tstream(jobs, pre=("v", vmk))
                barrier()
                busy = [None, None]
                hv_done = 0
                for ft in range(24):
                    b = ft % 2

                    def _(g, ft=ft, b=b, bz=busy[b]):
                        if bz is not None:
                            g.wait_ge(semh["pe"], bz)
                        inc(g.dma_start(out=WSL[b][:], in_=dr["w_g1"][:, ft * 1024:(ft + 1) * 1024]), "dg", 16)
                        inc(g.dma_start(out=WSU[b][:], in_=dr["w_u1"][:, ft * 1024:(ft + 1) * 1024]), "dg", 16)
                    GP(_)
                    dmk = sem["dg"]
                    for nh in range(2):
                        def _(t, b=b, nh=nh, dmk=dmk, hvd=hv_done):
                            t.wait_ge(semh["dg"], dmk)
                            if hvd > 0:
                                t.wait_ge(semh["v"], hvd)
                            last = None
                            for kt in range(8):
                                last = t.matmul(PM[0][:], lhsT=WSL[b][:, kt * 128:(kt + 1) * 128],
                                                rhs=XH[:, kt * 1024 + nh * 512: kt * 1024 + nh * 512 + 512],
                                                start=(kt == 0), stop=(kt == 7))
                            inc(last, "pe")
                            last = None
                            for kt in range(8):
                                last = t.matmul(PM[1][:], lhsT=WSU[b][:, kt * 128:(kt + 1) * 128],
                                                rhs=XH[:, kt * 1024 + nh * 512: kt * 1024 + nh * 512 + 512],
                                                start=(kt == 0), stop=(kt == 7))
                            inc(last, "pe")
                        TE(_)
                        pmk = sem["pe"]

                        def _(s, pmk=pmk):
                            s.wait_ge(semh["pe"], pmk)
                            s.activation(out=SIL[:], in_=PM[0][:], func=AF.Silu, scale=1.0 / 127.0)
                            s.copy(out=UPS[:], in_=PM[1][:])
                            inc(s.drain(), "sc")
                        SC_(_)
                        smk = sem["sc"]

                        def _(v, ft=ft, nh=nh, smk=smk):
                            v.wait_ge(semh["sc"], smk)
                            v.tensor_tensor(out=BIG[:, o_h + ft * 1024 + nh * 512:
                                                    o_h + ft * 1024 + nh * 512 + 512],
                                            in0=UPS[:], in1=SIL[:], op=AL.mult)
                            inc(v.drain(), "v")
                        VE(_)
                        hv_done = sem["v"]
                    busy[b] = sem["pe"]
                barrier()
                if finish(stages > (8.5 if half == 0 else 8.8)):
                    return nc, dbg_names
                res_done = 0
                for nh in range(2):
                    for ftg in range(2):
                        def _(t, nh=nh, ftg=ftg, rd=res_done, wdr=wd_ready):
                            t.wait_ge(semh["dg"], wdr)
                            if rd > 0:
                                t.wait_ge(semh["sc"], rd)
                            last = None
                            for kt in range(24):
                                for f4 in range(4):
                                    ftd = ftg * 4 + f4
                                    last = t.matmul(PM[f4][:],
                                                    lhsT=BIG[:, o_wd + (kt * 8 + ftd) * 128:
                                                             o_wd + (kt * 8 + ftd) * 128 + 128],
                                                    rhs=BIG[:, o_h + kt * 1024 + nh * 512:
                                                            o_h + kt * 1024 + nh * 512 + 512],
                                                    start=(kt == 0), stop=(kt == 23))
                            inc(last, "pe")
                        TE(_)
                        pmk = sem["pe"]

                        def _(s, pmk=pmk):
                            s.wait_ge(semh["pe"], pmk)
                            for f4 in range(4):
                                s.activation(out=DOF[:, f4 * 512:(f4 + 1) * 512], in_=PM[f4][:],
                                             func=AF.Identity, bias=0.0, scale=1.0 / 16129.0)
                            inc(s.drain(), "sc")
                        SC_(_)
                        vdo = sem["sc"]
                        res_done = vdo
                        jobs = []
                        for f4 in range(4):
                            for c in range(4):
                                it = half * 8 + nh * 4 + c
                                ftd = ftg * 4 + f4

                                def evs(s, chunk, strip):
                                    s.copy(out=strip, in_=chunk)

                                def evv(v, strip, it=it, ftd=ftd):
                                    v.tensor_tensor(out=X[:, it * 1024 + ftd * 128: it * 1024 + ftd * 128 + 128],
                                                    in0=X[:, it * 1024 + ftd * 128: it * 1024 + ftd * 128 + 128],
                                                    in1=strip, op=AL.add)
                                jobs.append((DOF[:, f4 * 512 + c * 128: f4 * 512 + c * 128 + 128],
                                             ("vp", evs, evv)))
                        tstream(jobs, pre=("sc", vdo))
                barrier()
                if finish(stages > (8.7 if half == 0 else 8.9)):
                    return nc, dbg_names

        def _(s):
            inc(s.dma_start(out=x_out.rearrange("(a p) b -> p a b", p=128),
                            in_=X[:].rearrange("p (a b) -> p a b", a=NT)), "dsy", 16)
            s.wait_ge(semh["dsy"], sem["dsy"])
        SY(_)
    return nc, dbg_names



def _host_prep(inputs):
    f32 = np.float32
    ins = {}

    def img_ft(wT, ktn, nf):
        return np.ascontiguousarray(
            wT.reshape(ktn, 128, nf, 128).transpose(1, 2, 0, 3).reshape(128, nf * ktn * 128))

    def img_nat(w, ntile, nfree):
        return np.ascontiguousarray(w.reshape(ntile, 128, nfree).transpose(1, 0, 2).reshape(128, ntile * nfree))

    def img_kt(wT, ktn, nfree):
        return np.ascontiguousarray(wT.reshape(ktn, 128, nfree).transpose(1, 0, 2).reshape(128, ktn * nfree))

    def colv(b, n):
        return np.ascontiguousarray(np.asarray(b, f32).reshape(n, 128).T)

    ins["w_kv"] = img_ft(np.asarray(inputs["L0_attn_kv_w"], f32), 8, 16)
    ins["w_q"] = img_ft(np.asarray(inputs["L0_attn_q_w"], f32), 8, 8)
    ins["w_o0"] = img_ft(np.asarray(inputs["L0_attn_out_w"], f32), 8, 8)
    ins["b_kv"] = colv(inputs["L0_attn_kv_b"], 16)
    ins["b_q"] = colv(inputs["L0_attn_q_b"], 8)
    ins["b_o0"] = colv(inputs["L0_attn_out_b"], 8)
    ins["kvn_w"] = colv(inputs["L0_attn_kvn_w"], 8)
    ins["kvn_b"] = colv(inputs["L0_attn_kvn_b"], 8)
    ins["qn_w"] = colv(inputs["L0_attn_qn_w"], 8)
    ins["qn_b"] = colv(inputs["L0_attn_qn_b"], 8)
    ins["ln_w"] = np.ascontiguousarray(np.asarray(inputs["L0_attn_ln_w"], f32).reshape(1, 64))
    ins["ln_b"] = np.ascontiguousarray(np.asarray(inputs["L0_attn_ln_b"], f32).reshape(1, 64))
    ins["rh_w"] = np.ascontiguousarray(np.asarray(inputs["L0_attn_rh_w"], f32).reshape(1, 1024))
    ins["rh_b"] = np.ascontiguousarray(np.asarray(inputs["L0_attn_rh_b"], f32).reshape(1, 1))
    ins["w_g0"] = img_nat(np.asarray(inputs["L0_ffn_w_gate"], f32), 22, 1024)
    ins["w_u0"] = img_nat(np.asarray(inputs["L0_ffn_w_up"], f32), 22, 1024)
    ins["w_d0"] = img_nat(np.asarray(inputs["L0_ffn_w_down"], f32), 8, 2816)
    ins["w_qkv1"] = img_ft(np.ascontiguousarray(np.asarray(inputs["L1_attn_Wqkv_w"], f32).T), 8, 24)
    ins["b_qkv1"] = colv(inputs["L1_attn_Wqkv_b"], 24)
    ins["w_o1"] = img_ft(np.ascontiguousarray(np.asarray(inputs["L1_attn_out_w"], f32).T), 8, 8)
    ins["b_o1"] = colv(inputs["L1_attn_out_b"], 8)
    wst = np.asarray(inputs["L1_attn_W_static"], f32)
    wst_img = np.zeros((128, 512), f32)
    for h in range(16):
        wst_img[(h % 2) * 64:(h % 2) * 64 + 64, (h // 2) * 64:(h // 2) * 64 + 64] = wst[h].T
    ins["w_st"] = np.ascontiguousarray(wst_img)

    def scan_img(a):
        b = np.asarray(a, f32).reshape(16, 32, 2, 64)
        return np.ascontiguousarray(b.transpose(2, 3, 0, 1).reshape(128, 512))
    ins["lam"] = scan_img(inputs["L1_attn_Lambda_raw"])
    ins["gam"] = scan_img(inputs["L1_attn_Gamma"])
    ins["w_g1"] = img_ft(np.ascontiguousarray(np.asarray(inputs["L1_ffn_w_gate"], np.int32).T), 8, 24)
    ins["w_u1"] = img_ft(np.ascontiguousarray(np.asarray(inputs["L1_ffn_w_up"], np.int32).T), 8, 24)
    ins["w_d1"] = img_kt(np.ascontiguousarray(np.asarray(inputs["L1_ffn_w_down"], np.int32).T), 24, 1024)
    return ins


def run(inputs, stages=STAGES, debug=DEBUG, cores=8):
    x = np.ascontiguousarray(np.asarray(inputs["L0_attn_x"], np.float32))
    ins = _host_prep(inputs)
    nc, dbg_names = build(stages=stages, debug=debug)
    in_maps = []
    for c in range(cores):
        m = dict(ins)
        m["x_in"] = np.ascontiguousarray(x[c])
        in_maps.append(m)
    res = run_bass_kernel_spmd(nc, in_maps, core_ids=list(range(cores)))
    outs = [np.asarray(r["x_out"]) for r in res.results]
    extra = [{k: np.asarray(r[k]) for k in r.keys() if k != "x_out"} for r in res.results]
    return np.stack(outs, 0), extra


def kernel(**inputs):
    out, _ = run(inputs, stages=99, debug=False, cores=8)
    return out.astype(np.float32)
